# version 04
# baseline (speedup 1.0000x reference)
import numpy as np
import ml_dtypes
import concourse.bass as bass
import concourse.mybir as mybir
from concourse.bass_utils import run_bass_kernel_spmd
from contextlib import ExitStack

F32 = mybir.dt.float32
BF16 = mybir.dt.bfloat16
U8 = mybir.dt.uint8
AF = mybir.ActivationFunctionType
ALU = mybir.AluOpType
AX = mybir.AxisListType

NCORES = 8
R = 1024
D = 4096
Hh = 11008
SCH = 512
NHC = 22
KPAD = 2176

BF = np.dtype(ml_dtypes.bfloat16)

NF4_CODE = np.array([-1.0, -0.6961928009986877, -0.5250730514526367,
    -0.39491748809814453, -0.28444138169288635, -0.18477343022823334,
    -0.09105003625154495, 0.0, 0.07958029955625534, 0.16093020141124725,
    0.24611230194568634, 0.33791524171829224, 0.44070982933044434,
    0.5626170039176941, 0.7229568362236023, 1.0], dtype=np.float32)
NF4_BOUNDS = ((NF4_CODE[:-1] + NF4_CODE[1:]) * np.float32(0.5)).astype(np.float32)
NF4_DELTA = np.diff(NF4_CODE).astype(np.float32)

ENGS = ("sync", "scalar", "vector", "tensor", "gpsimd")


class Sched:
    def __init__(self, nc, st):
        self.nc = nc
        self.q = {e: [] for e in ENGS}
        self.cnt = {}
        self.sems = {}
        self.st = st
        self.phase = 0
        self.meta = {e: [] for e in ENGS}

    def sem(self, name):
        if name not in self.sems:
            self.sems[name] = self.st.enter_context(self.nc.semaphore(name))
            self.cnt[name] = 0
        return self.sems[name]

    def op(self, eng, meth, *args, inc=None, inc_n=1, **kw):
        s = self.sem(inc) if inc else None

        def f(e, meth=meth, args=args, kw=kw, s=s, n=inc_n):
            inst = getattr(e, meth)(*args, **kw)
            if s is not None:
                inst.then_inc(s, n)
        self.q[eng].append(f)
        self.meta[eng].append(("inc", inc, inc_n, meth))
        if inc:
            self.cnt[inc] += inc_n
            return self.cnt[inc]
        return None

    def wait(self, eng, sem, val=None):
        s = self.sem(sem)
        v = self.cnt[sem] if val is None else val
        if v <= 0:
            return
        self.q[eng].append(lambda e, s=s, v=v: e.wait_ge(s, v))
        self.meta[eng].append(("wait", sem, v))

    def val(self, sem):
        self.sem(sem)
        return self.cnt[sem]

    def barrier(self):
        self.phase += 1
        for e in ENGS:
            self.op(e, "nop", inc=f"ph_{e}")
        for e in ENGS:
            for o in ENGS:
                if o != e:
                    self.wait(e, f"ph_{o}", self.cnt[f"ph_{o}"])

    def build(self):
        with self.nc.Block() as block:
            @block.sync
            def _(e):
                for f in self.q["sync"]:
                    f(e)

            @block.scalar
            def _(e):
                for f in self.q["scalar"]:
                    f(e)

            @block.vector
            def _(e):
                for f in self.q["vector"]:
                    f(e)

            @block.tensor
            def _(e):
                for f in self.q["tensor"]:
                    f(e)

            @block.gpsimd
            def _(e):
                for f in self.q["gpsimd"]:
                    f(e)


def build_nc(upto="full"):
    nc = bass.Bass("TRN2", target_bir_lowering=False, debug=False)

    x_in = nc.dram_tensor("x", [R, D], F32, kind="ExternalInput").ap()
    w1s = nc.dram_tensor("w1s", [Hh, 512], F32, kind="ExternalInput").ap()
    w2s = nc.dram_tensor("w2s", [Hh, 512], F32, kind="ExternalInput").ap()
    w3s = nc.dram_tensor("w3s", [512, Hh], F32, kind="ExternalInput").ap()
    cs_in = nc.dram_tensor("cs", [D, D], BF16, kind="ExternalInput").ap()
    icc_in = nc.dram_tensor("icc", [KPAD, D], BF16, kind="ExternalInput").ap()
    ico_in = nc.dram_tensor("ico", [KPAD, D], BF16, kind="ExternalInput").ap()
    ident_in = nc.dram_tensor("ident", [128, 128], BF16, kind="ExternalInput").ap()
    y_out = nc.dram_tensor("y", [R, D], F32, kind="ExternalOutput").ap()

    resid = nc.dram_tensor("resid", [R, D], F32).ap()
    ln_d = nc.dram_tensor("ln_d", [R, D], F32).ap()
    csb_d = nc.dram_tensor("csb_d", [R, D], BF16).ap()
    w1t_own = nc.dram_tensor("w1t_own", [512, Hh], BF16).ap()
    w2t_own = nc.dram_tensor("w2t_own", [512, Hh], BF16).ap()
    w3t_own = nc.dram_tensor("w3t_own", [Hh, 512], BF16).ap()
    w1t_full = nc.dram_tensor("w1t_full", [D, Hh], BF16, addr_space="Shared").ap()
    w2t_full = nc.dram_tensor("w2t_full", [D, Hh], BF16, addr_space="Shared").ap()
    w3t_g = nc.dram_tensor("w3t_g", [NCORES * Hh, 512], BF16, addr_space="Shared").ap()

    st = ExitStack()
    with st:
        def sbuf(name, shape, dt):
            return st.enter_context(nc.sbuf_tensor(name, shape, dt))

        POOLB = 204800
        pool = sbuf("pool", [128, POOLB], U8)[:]

        def bfv(off, cols):
            return pool[:, off:off + 2 * cols].bitcast(BF16)

        def f32v(off, cols):
            return pool[:, off:off + 4 * cols].bitcast(F32)

        ident = sbuf("identsb", [128, 128], BF16)[:]
        c_eps = sbuf("c_eps", [128, 1], F32)[:]
        c_nh = sbuf("c_nh", [128, 1], F32)[:]
        stat = sbuf("stat", [128, 48], F32)[:]
        mv = sbuf("mv", [128, 2], F32)[:]
        rstd = sbuf("rstd", [128, 1], F32)[:]
        m8 = sbuf("m8", [128, 8], F32)[:]
        wsm = sbuf("wsm", [128, 8], F32)[:]
        ssum = sbuf("ssum", [128, 1], F32)[:]
        srec = sbuf("srec", [128, 1], F32)[:]
        absm = sbuf("absm", [128, 22], F32)[:]
        recb = sbuf("recb", [128, 22], F32)[:]

        ps = st.enter_context(nc.psum_tensor("ps", [128, 4096], F32))[:]

        def bank(b, cols=512):
            return ps[:, 512 * b: 512 * b + cols]

        def ps_bf(f32col, bfcols):
            return ps[:, f32col:f32col + (bfcols + 1) // 2].bitcast(BF16)

        S = Sched(nc, st)

        S.op("vector", "memset", c_eps, 1e-5)
        S.op("vector", "memset", c_nh, -0.5, inc="vi")
        S.op("sync", "dma_start", out=ident, in_=ident_in, inc="li", inc_n=16)
        S.wait("tensor", "li")
        S.wait("scalar", "vi")
        S.wait("vector", "li")

        cur_set = [None]

        def act(out, in_, func, table, **kw):
            if cur_set[0] != table:
                S.op("scalar", "nop", cycle_cnt=4000)
                cur_set[0] = table
            return S.op("scalar", "activation", out, in_, func, **kw)

        DQ_IN = [0, 5632]
        DQ_NRM = 11264
        DQ_TMP = 16896
        DQ_ACC = 22528
        DQ_BF = [28160, 30976]
        DQ_WT = 33792
        wt44 = bfv(DQ_WT, 44032)

        dq_n = [0]

        def dq_tile(src_slice, tw, dst_evac_ap):
            b = dq_n[0]
            dq_n[0] += 1
            nblk = tw // 64
            nb128 = tw // 128
            vin = f32v(DQ_IN[b % 2], tw)
            dqb = bfv(DQ_BF[b % 2], tw)
            if b >= 2:
                S.wait("sync", "dq_vf", b - 1)
            S.op("sync", "dma_start", out=vin, in_=src_slice,
                 inc="dq_ld", inc_n=16)
            S.wait("vector", "dq_ld")
            S.op("vector", "tensor_reduce", absm[:, 0:nblk],
                 vin.rearrange("p (nb k) -> p nb k", k=64),
                 axis=AX.X, op=ALU.max, apply_absolute_value=True, inc="vss")
            S.wait("vector", "vss")
            S.op("vector", "reciprocal", recb[:, 0:nblk], absm[:, 0:nblk],
                 inc="vss")
            S.wait("vector", "vss")
            vn = f32v(DQ_NRM, tw)
            S.op("vector", "tensor_tensor",
                 vn.rearrange("p (nb k) -> p nb k", k=64),
                 vin.rearrange("p (nb k) -> p nb k", k=64),
                 recb[:, 0:nblk].unsqueeze(2).to_broadcast([128, nblk, 64]),
                 op=ALU.mult, inc="dq_vf")
            acc = f32v(DQ_ACC, tw)
            tmp = f32v(DQ_TMP, tw)
            BIG = 1e30
            S.op("vector", "memset", acc, float(NF4_CODE[15]))
            for j in range(15):
                S.op("vector", "tensor_scalar", tmp, vn,
                     float(NF4_BOUNDS[j]), float(BIG - float(NF4_CODE[j])),
                     op0=ALU.is_gt, op1=ALU.mult)
                S.op("vector", "scalar_tensor_tensor", acc, tmp,
                     float(NF4_CODE[j]), acc, op0=ALU.add, op1=ALU.min)
            if b >= 2:
                S.wait("vector", "dq_pe", b - 1)
            S.op("vector", "tensor_tensor",
                 dqb.rearrange("p (nb k) -> p nb k", k=64),
                 acc.rearrange("p (nb k) -> p nb k", k=64),
                 absm[:, 0:nblk].unsqueeze(2).to_broadcast([128, nblk, 64]),
                 op=ALU.mult, inc="dq_v")
            S.wait("tensor", "dq_v", S.val("dq_v"))
            S.wait("tensor", "dq_se", b)
            for j in range(nb128):
                S.op("tensor", "transpose",
                     ps_bf(3072 + 64 * j, 128),
                     dqb[:, 128 * j:128 * (j + 1)], ident,
                     inc="dq_pe" if j == nb128 - 1 else None)
            S.wait("scalar", "dq_pe", S.val("dq_pe"))
            S.op("scalar", "copy", dst_evac_ap,
                 ps_bf(3072, nb128 * 128).rearrange("p (j h) -> p j h", j=nb128),
                 inc="dq_se")

        wlist = ((w1s, w1t_own), (w2s, w2t_own))
        if upto == "dq_w1":
            wlist = ((w1s, w1t_own),)
        for wi, (wsrc, wown) in enumerate(wlist):
            if wi == 1:
                S.wait("scalar", "dq_wr", S.val("dq_wr"))
            for t in range(86):
                dst = wt44.rearrange("p (j h) -> p j h", j=4)[
                    :, :, 128 * t:128 * (t + 1)]
                dq_tile(wsrc[128 * t:128 * (t + 1), :], 512, dst)
            S.wait("gpsimd", "dq_se", S.val("dq_se"))
            S.op("gpsimd", "dma_start",
                 out=wown.rearrange("(j p) h -> p j h", p=128),
                 in_=wt44.rearrange("p (j h) -> p j h", j=4),
                 inc="dq_wr", inc_n=16)
        S.wait("scalar", "dq_wr", S.val("dq_wr"))
        if upto == "dq_w1":
            w3_tw = []
        else:
            w3_tw = [1408] * 7 + [1152]
        w3_off = [sum(w3_tw[:i]) for i in range(8)]
        for dt in range(4 if w3_tw else 0):
            for sc in range(8):
                tw = w3_tw[sc]
                hoff = w3_off[sc]
                dst = wt44.rearrange("p (ht d) -> p ht d", d=512)[
                    :, hoff // 128:(hoff + tw) // 128, 128 * dt:128 * (dt + 1)]
                dq_tile(w3s[128 * dt:128 * (dt + 1), hoff:hoff + tw], tw, dst)
        if w3_tw:
            S.wait("gpsimd", "dq_se", S.val("dq_se"))
            S.op("gpsimd", "dma_start",
                 out=w3t_own.rearrange("(ht p) d -> p ht d", p=128),
                 in_=wt44.rearrange("p (ht d) -> p ht d", d=512),
                 inc="dq_wr", inc_n=16)
            S.wait("gpsimd", "dq_wr", S.val("dq_wr"))
        if upto not in ("dq_ng", "dq_w1"):
            rg = [list(range(NCORES))]
            S.op("gpsimd", "collective_compute", "AllGather", ALU.bypass, rg,
                 [w1t_own], [w1t_full], inc="cc")
            S.op("gpsimd", "collective_compute", "AllGather", ALU.bypass, rg,
                 [w2t_own], [w2t_full], inc="cc")
            S.op("gpsimd", "collective_compute", "AllGather", ALU.bypass, rg,
                 [w3t_own], [w3t_g], inc="cc")
            S.wait("gpsimd", "cc", S.val("cc"))
        S.barrier()

        if upto in ("dq", "dq_ng", "dq_w1"):
            if upto == "dq_ng":
                w1t_full, w2t_full, w3t_g = w1t_own, w2t_own, w3t_own
            elif upto == "dq_w1":
                w1t_full, w2t_full, w3t_g = w1t_own, w1t_own, w1t_own
            dbg = f32v(0, D)
            dbgb = bfv(16384, D)
            row2 = 512 if upto == "dq" else 128
            for i, src in enumerate((w1t_full[0:128, 0:D],
                                     w2t_full[0:128, 0:D],
                                     w3t_g[0:128, 0:512],
                                     w1t_full[row2:row2 + 128, 0:D])):
                cols = src.shape[-1]
                S.wait("sync", "dbg_v", S.val("dbg_v"))
                S.op("sync", "dma_start", out=dbgb[:, 0:cols], in_=src,
                     inc="dbg_l", inc_n=16)
                S.wait("vector", "dbg_l")
                S.wait("vector", "dbg_s", S.val("dbg_s"))
                S.op("vector", "tensor_copy", dbg[:, 0:cols], dbgb[:, 0:cols],
                     inc="dbg_v")
                S.wait("scalar", "dbg_v")
                S.op("scalar", "dma_start",
                     out=y_out[128 * i:128 * (i + 1), 0:cols],
                     in_=dbg[:, 0:cols], inc="dbg_s", inc_n=16)
            S.wait("scalar", "dbg_s", S.val("dbg_s"))
            S.barrier()
            S.build()
            return nc

        ln_n = [0]
        LNF_OFF = [0]

        def emit_ln(src_ap, r0, xoff, lnbf_off, f32_dram=None):
            n = ln_n[0]
            ln_n[0] += 1
            xt = f32v(xoff, D)
            if n >= 2:
                S.wait("scalar", "ln_bf", S.val("ln_bf") - 1)
            S.op("scalar", "dma_start", out=xt, in_=src_ap[r0:r0 + 128, :],
                 inc="xl", inc_n=16)
            S.wait("vector", "xl")
            for gg in range(8):
                S.op("vector", "bn_stats", stat[:, 6 * gg:6 * gg + 6],
                     xt[:, 512 * gg:512 * (gg + 1)],
                     inc="vss" if gg == 7 else None)
            S.wait("vector", "vss")
            S.op("vector", "bn_aggr", mv,
                 stat.rearrange("p (g k) -> p g k", k=3), inc="ln_v")
            S.wait("scalar", "ln_v")
            act(rstd, mv[:, 1:2], AF.Ln, "ln", bias=c_eps)
            act(rstd, rstd, AF.Exp, "ln", scale=c_nh, inc="ln_s")
            S.wait("vector", "ln_s")
            S.wait("vector", "tr_pe", S.val("tr_pe"))
            lnb = bfv(lnbf_off, D)
            S.op("vector", "tensor_scalar", lnb, xt, mv[:, 0:1], rstd,
                 op0=ALU.subtract, op1=ALU.mult, inc="ln_bf")
            if f32_dram is not None:
                lnf = f32v(LNF_OFF[0], D)
                if n >= 1:
                    S.wait("vector", "lnst", S.val("lnst"))
                S.op("vector", "tensor_scalar", lnf, xt, mv[:, 0:1], rstd,
                     op0=ALU.subtract, op1=ALU.mult, inc="ln_f")
                S.wait("scalar", "ln_f")
                S.op("scalar", "dma_start", out=f32_dram[r0:r0 + 128, :],
                     in_=lnf, inc="lnst", inc_n=16)

        tr_n = [0]

        def emit_lnT(lnbf_off, lnT_off, blk_cols, col0):
            lnb = bfv(lnbf_off, D)
            lnT = bfv(lnT_off, 32 * blk_cols)
            S.wait("tensor", "ln_bf", S.val("ln_bf"))
            for g in range(8):
                gidx = tr_n[0]
                tr_n[0] += 1
                pp = gidx % 2
                if gidx >= 2:
                    S.wait("tensor", "tr_se", gidx - 1)
                for u in range(4):
                    dtj = g * 4 + u
                    S.op("tensor", "transpose",
                         ps_bf(512 * (6 + pp) + 64 * u, 128),
                         lnb[:, 128 * dtj:128 * (dtj + 1)], ident,
                         inc="tr_pe" if u == 3 else None)
                S.wait("scalar", "tr_pe", S.val("tr_pe"))
                dst = lnT.rearrange("p (dt c) -> p dt c", dt=32)[
                    :, 4 * g:4 * (g + 1), col0:col0 + 128]
                S.op("scalar", "copy", dst,
                     ps_bf(512 * (6 + pp), 512).rearrange(
                         "p (u c) -> p u c", u=4),
                     inc="tr_se")

        A_LNT = 0
        A_HAT = 32768
        A_WS1 = 120832
        A_WS2 = 153600
        A_HC = 186368
        A_HCS = 190464

        def emit_ffn_chunk(src_ap, dst_ap, s0):
            for sm in ("ev_s", "ev_u", "mm3_ev", "htr_pe"):
                S.wait("tensor", sm, S.val(sm))
            for stl in range(4):
                xoff = A_HAT + (stl % 2) * 16384
                lnboff = A_HAT + 32768 + (stl % 2) * 8192
                emit_ln(src_ap, s0 + 128 * stl, xoff, lnboff)
                emit_lnT(lnboff, A_LNT, 512, 128 * stl)
            S.wait("tensor", "tr_se", S.val("tr_se"))
            lnT = bfv(A_LNT, 32 * 512)
            haT = bfv(A_HAT, 44032)
            slab1 = bfv(A_WS1, 32 * 512)
            slab2 = bfv(A_WS2, 32 * 512)

            ev_s0 = S.val("ev_s")
            ev_u0 = S.val("ev_u")
            hpe0 = S.val("htr_pe")
            hse0 = S.val("htr_se")
            gp0 = S.val("gu_pe")
            wf10 = S.val("wf1")

            def w_src(full, h0, hcw):
                return full.rearrange("(dt p) h -> p dt h", p=128)[
                    :, :, h0:h0 + hcw]

            S.wait("sync", "w3f", S.val("w3f"))
            S.wait("sync", "mm3_ev", S.val("mm3_ev"))

            def tr_wave(g):
                hcb, stt = g // 4, g % 4
                hcw = 512 if hcb < NHC - 1 else 256
                nbt = hcw // 128
                hbuf = bfv(A_HC + 1024 * (g % 4), 512)[:, 0:hcw]
                S.wait("tensor", "ev_u", ev_u0 + g + 1)
                if g >= 2:
                    S.wait("tensor", "htr_se", hse0 + g - 1)
                pp = g % 2
                for j in range(nbt):
                    S.op("tensor", "transpose",
                         ps_bf(512 * (6 + pp) + 64 * j, 128),
                         hbuf[:, 128 * j:128 * (j + 1)], ident,
                         inc="htr_pe" if j == nbt - 1 else None)
                S.wait("scalar", "htr_pe", S.val("htr_pe"))
                for j in range(nbt):
                    ht = hcb * 4 + j
                    S.op("scalar", "copy",
                         haT[:, (ht * 4 + stt) * 128:(ht * 4 + stt) * 128 + 128],
                         ps_bf(512 * (6 + pp) + 64 * j, 128),
                         inc="htr_se" if j == nbt - 1 else None)

            for hc in range(NHC):
                hcw = 512 if hc < NHC - 1 else 256
                h0 = 512 * hc
                if hc > 0:
                    S.wait("sync", "wf1", wf10 + hc)
                S.op("sync", "dma_start",
                     out=slab1[:, 0:32 * hcw].rearrange(
                         "p (d h) -> p d h", d=32),
                     in_=w_src(w1t_full, h0, hcw), inc="ws1", inc_n=16)
                if hc > 0:
                    S.wait("sync", "gu_pe", gp0 + 4 * hc)
                S.op("sync", "dma_start",
                     out=slab2[:, 0:32 * hcw].rearrange(
                         "p (d h) -> p d h", d=32),
                     in_=w_src(w2t_full, h0, hcw), inc="ws2", inc_n=16)
                S.wait("tensor", "ws1", S.val("ws1"))
                for stt in range(4):
                    if hc > 0:
                        S.wait("tensor", "ev_s", ev_s0 + 4 * (hc - 1) + stt + 1)
                    for dt in range(32):
                        S.op("tensor", "matmul", bank(stt, hcw),
                             lnT[:, dt * 512 + stt * 128:
                                 dt * 512 + stt * 128 + 128],
                             slab1[:, dt * hcw:(dt + 1) * hcw],
                             start=(dt == 0), stop=(dt == 31),
                             skip_group_check=True,
                             inc="wf1" if (stt == 3 and dt == 31) else None)
                S.wait("tensor", "ws2", S.val("ws2"))
                for stt in range(4):
                    g = hc * 4 + stt
                    if g >= 2:
                        S.wait("tensor", "ev_u", ev_u0 + g - 1)
                    bu = 4 + g % 2
                    for dt in range(32):
                        S.op("tensor", "matmul", bank(bu, hcw),
                             lnT[:, dt * 512 + stt * 128:
                                 dt * 512 + stt * 128 + 128],
                             slab2[:, dt * hcw:(dt + 1) * hcw],
                             start=(dt == 0), stop=(dt == 31),
                             skip_group_check=True,
                             inc="gu_pe" if dt == 31 else None)
                    S.wait("scalar", "gu_pe", gp0 + g + 1)
                    if g >= 2:
                        S.wait("scalar", "ev_u", ev_u0 + g - 1)
                    hcs = bfv(A_HCS + 1024 * (g % 2), 512)[:, 0:hcw]
                    act(hcs, bank(stt, hcw), AF.Silu, "silu", inc="ev_s")
                    S.wait("vector", "ev_s", ev_s0 + g + 1)
                    if g >= 4:
                        S.wait("vector", "htr_pe", hpe0 + g - 3)
                    hbuf = bfv(A_HC + 1024 * (g % 4), 512)[:, 0:hcw]
                    S.op("vector", "tensor_tensor", hbuf, hcs,
                         bank(bu, hcw), op=ALU.mult, inc="ev_u")
                    if g >= 4:
                        tr_wave(g - 4)
            for g in range(4 * NHC - 4, 4 * NHC):
                tr_wave(g)
            S.wait("tensor", "htr_se", S.val("htr_se"))

            m3ev0 = S.val("mm3_ev")
            w3f0 = S.val("w3f")
            xo0 = S.val("xo_st")
            xq = [f32v(A_WS1 + 4096 * i, 1024) for i in range(2)]
            xo = [f32v(A_WS2 + 4096 * i, 1024) for i in range(2)]
            w3sl = [bfv(A_WS1 + 16384 + 2048 * i, 1024) for i in range(2)]
            w3gv = w3t_g.rearrange("(rk hh) d -> rk hh d", rk=NCORES)
            S.wait("sync", "gu_pe", S.val("gu_pe"))
            for q in range(4):
                for ht in range(86):
                    g3 = q * 86 + ht
                    sl = w3sl[g3 % 2]
                    if g3 >= 2:
                        S.wait("sync", "w3f", w3f0 + g3 - 1)
                    S.op("sync", "dma_start",
                         out=sl.rearrange("p (rk d) -> p rk d", rk=2),
                         in_=w3gv[2 * q:2 * q + 2,
                                  128 * ht:128 * (ht + 1), :].rearrange(
                                      "rk p d -> p rk d"),
                         inc="w3l", inc_n=16)
                    S.wait("tensor", "w3l", S.val("w3l"))
                    if ht == 0 and q > 0:
                        S.wait("tensor", "mm3_ev", m3ev0 + 4 * q)
                    for stt in range(4):
                        for dc in range(2):
                            b = stt * 2 + dc
                            S.op("tensor", "matmul", bank(b),
                                 haT[:, (ht * 4 + stt) * 128:
                                     (ht * 4 + stt) * 128 + 128],
                                 sl[:, 512 * dc:512 * (dc + 1)],
                                 start=(ht == 0), stop=(ht == 85),
                                 skip_group_check=True,
                                 inc="w3f" if (stt == 3 and dc == 1) else None)
                for stt in range(4):
                    g4 = q * 4 + stt
                    bb = g4 % 2
                    if g4 >= 2:
                        S.wait("scalar", "mm3_ev", m3ev0 + g4 - 1)
                    S.op("scalar", "dma_start", out=xq[bb],
                         in_=src_ap[s0 + 128 * stt:s0 + 128 * (stt + 1),
                                    1024 * q:1024 * (q + 1)],
                         inc="xq", inc_n=16)
                    S.wait("vector", "xq", S.val("xq"))
                    S.wait("vector", "w3f", w3f0 + 86 * (q + 1))
                    if g4 >= 2:
                        S.wait("vector", "xo_st", xo0 + 16 * (g4 - 1))
                    for dc in range(2):
                        b = stt * 2 + dc
                        S.op("vector", "scalar_tensor_tensor",
                             xo[bb][:, 512 * dc:512 * (dc + 1)],
                             bank(b), 0.5,
                             xq[bb][:, 512 * dc:512 * (dc + 1)],
                             op0=ALU.mult, op1=ALU.add,
                             inc="mm3_ev" if dc == 1 else None)
                    S.wait("scalar", "mm3_ev", S.val("mm3_ev"))
                    S.op("scalar", "dma_start",
                         out=dst_ap[s0 + 128 * stt:s0 + 128 * (stt + 1),
                                    1024 * q:1024 * (q + 1)],
                         in_=xo[bb], inc="xo_st", inc_n=16)
            S.wait("scalar", "xo_st", S.val("xo_st"))

        def emit_ffn(src_ap, dst_ap):
            for c in range(2):
                emit_ffn_chunk(src_ap, dst_ap, c * SCH)

        emit_ffn(x_in, resid)
        S.barrier()

        AT_LNT = 0
        AT_X = 65536
        AT_SLAB = 131072
        LNF_OFF[0] = 114688

        for rt in range(8):
            xoff = AT_X + (rt % 2) * 16384
            lnboff = AT_X + 32768 + (rt % 2) * 8192
            emit_ln(resid, 128 * rt, xoff, lnboff, f32_dram=ln_d)
            emit_lnT(lnboff, AT_LNT, 1024, 128 * rt)
        S.wait("tensor", "tr_se", S.val("tr_se"))
        lnTa = bfv(AT_LNT, 32 * 1024)

        S.wait("sync", "lnst", S.val("lnst"))
        cs_sl = [bfv(AT_SLAB + 32768 * i, 32 * 512) for i in range(2)]
        fst0 = S.val("fwd_st")
        fev0 = S.val("fwd_ev")
        fpe0 = S.val("fwd_pe")
        for kc in range(8):
            sl = cs_sl[kc % 2]
            if kc >= 2:
                S.wait("sync", "fwd_pe", fpe0 + 8 * (kc - 1))
            S.op("sync", "dma_start",
                 out=sl.rearrange("p (d k) -> p d k", d=32),
                 in_=cs_in.rearrange("(dt p) k -> p dt k", p=128)[
                     :, :, 512 * kc:512 * (kc + 1)],
                 inc="csl", inc_n=16)
            S.wait("tensor", "csl", S.val("csl"))
            for rt in range(8):
                gf = kc * 8 + rt
                if kc > 0:
                    S.wait("tensor", "fwd_ev", fev0 + gf - 7)
                for dt in range(32):
                    S.op("tensor", "matmul", bank(rt),
                         lnTa[:, dt * 1024 + rt * 128:
                              dt * 1024 + rt * 128 + 128],
                         sl[:, dt * 512:(dt + 1) * 512],
                         start=(dt == 0), stop=(dt == 31),
                         skip_group_check=True,
                         inc="fwd_pe" if dt == 31 else None)
                S.wait("vector", "fwd_pe", S.val("fwd_pe"))
                evo = bfv(AT_X + 1024 * (gf % 2), 512)
                if gf >= 2:
                    S.wait("vector", "fwd_st", fst0 + 16 * (gf - 1))
                S.op("vector", "tensor_copy", evo, bank(rt), inc="fwd_ev")
                S.wait("scalar", "fwd_ev", S.val("fwd_ev"))
                S.op("scalar", "dma_start",
                     out=csb_d[128 * rt:128 * (rt + 1),
                               512 * kc:512 * (kc + 1)],
                     in_=evo, inc="fwd_st", inc_n=16)
        S.wait("scalar", "fwd_st", S.val("fwd_st"))

        AT_P = 0
        AT_PT = 34816
        AT_CSLD = 140288
        Pv = bfv(AT_P, 8 * KPAD)
        for rt in range(8):
            tl = bfv(AT_CSLD + 8192 * (rt % 2), D)
            if rt >= 2:
                S.wait("scalar", "p_v", S.val("p_v") - 1)
            S.op("scalar", "dma_start", out=tl,
                 in_=csb_d[128 * rt:128 * (rt + 1), :], inc="pl", inc_n=16)
            S.wait("vector", "pl")
            Pr = Pv[:, rt * KPAD:(rt + 1) * KPAD]
            S.op("vector", "memset", Pr[:, 2049:KPAD], 0.0)
            S.op("vector", "tensor_tensor", Pr[:, 0:2049], tl[:, 0:2049],
                 tl[:, 0:2049], op=ALU.mult)
            ptmp = f32v(AT_CSLD + 16384, 2047)
            S.op("vector", "tensor_tensor", ptmp, tl[:, 2049:4096],
                 tl[:, 2049:4096], op=ALU.mult)
            S.op("vector", "tensor_tensor", Pr[:, 1:2048], Pr[:, 1:2048],
                 ptmp, op=ALU.add, inc="p_v")
        PT = bfv(AT_PT, 17 * 1024)
        S.wait("tensor", "p_v", S.val("p_v"))
        pt0 = S.val("pt_se")
        tn = [0]
        for rt in range(8):
            for kt in range(17):
                i = tn[0]
                tn[0] += 1
                pp = i % 2
                if i >= 2:
                    S.wait("tensor", "pt_se", pt0 + i - 1)
                S.op("tensor", "transpose",
                     ps_bf(512 * (6 + pp), 128),
                     Pv[:, rt * KPAD + 128 * kt: rt * KPAD + 128 * (kt + 1)],
                     ident, inc="pt_pe")
                S.wait("scalar", "pt_pe", S.val("pt_pe"))
                S.op("scalar", "copy",
                     PT[:, kt * 1024 + rt * 128: kt * 1024 + rt * 128 + 128],
                     ps_bf(512 * (6 + pp), 128), inc="pt_se")
        S.wait("tensor", "pt_se", S.val("pt_se"))

        AT_CORR = 69632
        AT_ISL = 135168
        corr = bfv(AT_CORR, 8 * D)
        isl = [bfv(AT_ISL + 17408 * i, 17 * 512) for i in range(2)]
        S.wait("sync", "fwd_pe", S.val("fwd_pe"))
        S.wait("sync", "p_v", S.val("p_v"))
        cev0 = S.val("cor_ev")
        cpe0 = S.val("cor_pe")
        for tc in range(5):
            sl = isl[tc % 2]
            if tc >= 2:
                S.wait("sync", "cor_pe", cpe0 + 8 * (tc - 1))
            S.op("sync", "dma_start",
                 out=sl.rearrange("p (k t) -> p k t", k=17),
                 in_=icc_in.rearrange("(kt p) t -> p kt t", p=128)[
                     :, :, 512 * tc:512 * (tc + 1)],
                 inc="icl", inc_n=16)
            S.wait("tensor", "icl", S.val("icl"))
            for rt in range(8):
                gc = tc * 8 + rt
                if tc > 0:
                    S.wait("tensor", "cor_ev", cev0 + gc - 7)
                for kt in range(17):
                    S.op("tensor", "matmul", bank(rt),
                         PT[:, kt * 1024 + rt * 128:
                            kt * 1024 + rt * 128 + 128],
                         sl[:, kt * 512:(kt + 1) * 512],
                         start=(kt == 0), stop=(kt == 16),
                         skip_group_check=True,
                         inc="cor_pe" if kt == 16 else None)
                S.wait("vector", "cor_pe", S.val("cor_pe"))
                S.op("vector", "tensor_copy",
                     corr[:, rt * D + 512 * tc: rt * D + 512 * (tc + 1)],
                     bank(rt), inc="cor_ev")
        for rt in range(8):
            cr = corr[:, rt * D:(rt + 1) * D]
            S.op("vector", "tensor_copy", cr[:, 2560:4096],
                 cr[:, 1536:0:-1], inc="cor_ev")

        AT_A = 0
        Av = bfv(AT_A, 8 * D)
        atmp = bfv(AT_ISL, D)
        S.wait("vector", "cor_ev", S.val("cor_ev"))
        for rt in range(8):
            cr = corr[:, rt * D:(rt + 1) * D]
            S.op("vector", "max", out=m8, in_=cr, inc="vss")
            S.wait("vector", "vss")
            S.op("vector", "tensor_scalar", wsm, m8, m8[:, 0:1], None,
                 op0=ALU.subtract, inc="sm_v")
            S.wait("scalar", "sm_v")
            act(wsm, wsm, AF.Exp, "ln", inc="sm_s")
            S.wait("vector", "sm_s")
            S.op("vector", "tensor_reduce", ssum, wsm, axis=AX.X,
                 op=ALU.add, inc="vss")
            S.wait("vector", "vss")
            S.op("vector", "reciprocal", srec, ssum, inc="vss")
            S.wait("vector", "vss")
            S.op("vector", "tensor_scalar", wsm, wsm, srec, None,
                 op0=ALU.mult, inc="vss")
            S.wait("vector", "vss")
            Ar = Av[:, rt * D:(rt + 1) * D]
            S.op("vector", "tensor_scalar", Ar, cr, m8[:, 0:1], wsm[:, 0:1],
                 op0=ALU.is_equal, op1=ALU.mult)
            for k in range(1, 8):
                S.op("vector", "tensor_scalar", atmp, cr, m8[:, k:k + 1],
                     wsm[:, k:k + 1], op0=ALU.is_equal, op1=ALU.mult)
                S.op("vector", "tensor_tensor", Ar, Ar, atmp, op=ALU.add,
                     inc="a_v" if k == 7 else None)

        AT_AT = AT_CORR
        ATv = bfv(AT_AT, 32 * 1024)
        S.wait("tensor", "a_v", S.val("a_v"))
        at0 = S.val("at_se")
        tn2 = [0]
        for rt in range(8):
            for jt in range(32):
                i = tn2[0]
                tn2[0] += 1
                pp = i % 2
                if i >= 2:
                    S.wait("tensor", "at_se", at0 + i - 1)
                S.op("tensor", "transpose",
                     ps_bf(512 * (6 + pp), 128),
                     Av[:, rt * D + 128 * jt: rt * D + 128 * (jt + 1)], ident,
                     inc="at_pe")
                S.wait("scalar", "at_pe", S.val("at_pe"))
                S.op("scalar", "copy",
                     ATv[:, jt * 1024 + rt * 128: jt * 1024 + rt * 128 + 128],
                     ps_bf(512 * (6 + pp), 128), inc="at_se")
        S.wait("tensor", "at_se", S.val("at_se"))

        casa = bfv(0, 8 * D)
        cs_sl2 = [bfv(AT_ISL + 32768 * i, 32 * 512) for i in range(2)]
        S.wait("sync", "cor_pe", S.val("cor_pe"))
        S.wait("sync", "a_v", S.val("a_v"))
        aev0 = S.val("ah_ev")
        ape0 = S.val("ah_pe")
        for kc in range(8):
            sl = cs_sl2[kc % 2]
            if kc >= 2:
                S.wait("sync", "ah_pe", ape0 + 8 * (kc - 1))
            S.op("sync", "dma_start",
                 out=sl.rearrange("p (d k) -> p d k", d=32),
                 in_=cs_in.rearrange("(dt p) k -> p dt k", p=128)[
                     :, :, 512 * kc:512 * (kc + 1)],
                 inc="cs2l", inc_n=16)
            S.wait("tensor", "cs2l", S.val("cs2l"))
            for rt in range(8):
                ga = kc * 8 + rt
                if kc > 0:
                    S.wait("tensor", "ah_ev", aev0 + ga - 7)
                for jt in range(32):
                    S.op("tensor", "matmul", bank(rt),
                         ATv[:, jt * 1024 + rt * 128:
                             jt * 1024 + rt * 128 + 128],
                         sl[:, jt * 512:(jt + 1) * 512],
                         start=(jt == 0), stop=(jt == 31),
                         skip_group_check=True,
                         inc="ah_pe" if jt == 31 else None)
                S.wait("vector", "ah_pe", S.val("ah_pe"))
                S.op("vector", "tensor_copy",
                     casa[:, rt * D + 512 * kc: rt * D + 512 * (kc + 1)],
                     bank(rt), inc="ah_ev")
        S.wait("vector", "ah_ev", S.val("ah_ev"))

        AT_DR = 69632
        AT_DI = 69632 + 34816
        Dr = bfv(AT_DR, 8 * KPAD)
        Di = bfv(AT_DI, 8 * KPAD)
        DT_TL = 140288
        dtmp = f32v(156672, 2047)
        S.wait("scalar", "ah_pe", S.val("ah_pe"))
        for rt in range(8):
            tl = bfv(DT_TL + 8192 * (rt % 2), D)
            if rt >= 2:
                S.wait("scalar", "d_v", S.val("d_v") - 1)
            S.op("scalar", "dma_start", out=tl,
                 in_=csb_d[128 * rt:128 * (rt + 1), :], inc="dl", inc_n=16)
            S.wait("vector", "dl")
            ca = casa[:, rt * D:(rt + 1) * D]
            drr = Dr[:, rt * KPAD:(rt + 1) * KPAD]
            dii = Di[:, rt * KPAD:(rt + 1) * KPAD]
            S.op("vector", "memset", drr[:, 2049:KPAD], 0.0)
            S.op("vector", "memset", dii[:, 2048:KPAD], 0.0)
            S.op("vector", "memset", dii[:, 0:1], 0.0)
            S.op("vector", "tensor_tensor", drr[:, 0:2049], tl[:, 0:2049],
                 ca[:, 0:2049], op=ALU.mult)
            S.op("vector", "tensor_tensor", dtmp, tl[:, 2049:4096],
                 ca[:, 2049:4096], op=ALU.mult)
            S.op("vector", "tensor_tensor", drr[:, 1:2048], drr[:, 1:2048],
                 dtmp, op=ALU.add)
            S.op("vector", "tensor_tensor", drr[:, 0:2049], drr[:, 0:2049],
                 tl[:, 0:2049], op=ALU.subtract)
            S.op("vector", "tensor_tensor", dii[:, 1:2048], tl[:, 1:2048],
                 ca[:, 2049:4096], op=ALU.mult)
            S.op("vector", "tensor_tensor", dtmp, tl[:, 2049:4096],
                 ca[:, 1:2048], op=ALU.mult)
            S.op("vector", "tensor_tensor", dii[:, 1:2048], dii[:, 1:2048],
                 dtmp, op=ALU.subtract)
            S.op("vector", "tensor_tensor", dii[:, 1:2048], dii[:, 1:2048],
                 tl[:, 2049:4096], op=ALU.add, inc="d_v")

        DrT = bfv(0, 17 * 1024)
        DiT = bfv(34816, 17 * 1024)
        S.wait("tensor", "d_v", S.val("d_v"))
        dt0 = S.val("dt_se")
        tn3 = [0]
        for src, dstv in ((Dr, DrT), (Di, DiT)):
            for rt in range(8):
                for kt in range(17):
                    i = tn3[0]
                    tn3[0] += 1
                    pp = i % 2
                    if i >= 2:
                        S.wait("tensor", "dt_se", dt0 + i - 1)
                    S.op("tensor", "transpose",
                         ps_bf(512 * (6 + pp), 128),
                         src[:, rt * KPAD + 128 * kt:
                             rt * KPAD + 128 * (kt + 1)], ident,
                         inc="dt_pe")
                    S.wait("scalar", "dt_pe", S.val("dt_pe"))
                    S.op("scalar", "copy",
                         dstv[:, kt * 1024 + rt * 128:
                              kt * 1024 + rt * 128 + 128],
                         ps_bf(512 * (6 + pp), 128), inc="dt_se")
        S.wait("tensor", "dt_se", S.val("dt_se"))

        AT_I2 = 69632
        i2c = [bfv(AT_I2 + 17408 * i, 17 * 512) for i in range(2)]
        i2o = [bfv(AT_I2 + 34816 + 17408 * i, 17 * 512) for i in range(2)]
        S.wait("sync", "dt_pe", S.val("dt_pe"))
        AT_XT = 140288
        xt2 = f32v(AT_XT, D)
        lt2 = f32v(AT_XT + 16384, D)
        ot2 = f32v(AT_XT + 32768, D)
        crev0 = S.val("cr_ev")
        crpe0 = S.val("cr_pe")
        for rt in range(8):
            S.wait("scalar", "cr_ev", S.val("cr_ev"))
            S.op("scalar", "dma_start", out=xt2,
                 in_=resid[128 * rt:128 * (rt + 1), :], inc="cx", inc_n=16)
            S.op("scalar", "dma_start", out=lt2,
                 in_=ln_d[128 * rt:128 * (rt + 1), :], inc="cx", inc_n=16)
            for tc in range(8):
                gi = rt * 8 + tc
                slc = i2c[tc % 2]
                slo = i2o[tc % 2]
                if gi >= 2:
                    S.wait("sync", "cr_pe", crpe0 + gi - 1)
                S.op("sync", "dma_start",
                     out=slc.rearrange("p (k t) -> p k t", k=17),
                     in_=icc_in.rearrange("(kt p) t -> p kt t", p=128)[
                         :, :, 512 * tc:512 * (tc + 1)],
                     inc="i2l", inc_n=16)
                S.op("sync", "dma_start",
                     out=slo.rearrange("p (k t) -> p k t", k=17),
                     in_=ico_in.rearrange("(kt p) t -> p kt t", p=128)[
                         :, :, 512 * tc:512 * (tc + 1)],
                     inc="i2l", inc_n=16)
                S.wait("tensor", "i2l", S.val("i2l"))
                b = tc
                if rt > 0:
                    S.wait("tensor", "cr_ev", crev0 + gi - 7)
                for kt in range(17):
                    S.op("tensor", "matmul", bank(b),
                         DrT[:, kt * 1024 + rt * 128:
                             kt * 1024 + rt * 128 + 128],
                         slc[:, kt * 512:(kt + 1) * 512],
                         start=(kt == 0), stop=False, skip_group_check=True)
                for kt in range(17):
                    S.op("tensor", "matmul", bank(b),
                         DiT[:, kt * 1024 + rt * 128:
                             kt * 1024 + rt * 128 + 128],
                         slo[:, kt * 512:(kt + 1) * 512],
                         start=False, stop=(kt == 16), skip_group_check=True,
                         inc="cr_pe" if kt == 16 else None)
                S.wait("vector", "cr_pe", S.val("cr_pe"))
                if tc == 0:
                    S.wait("vector", "cx", S.val("cx"))
                    S.wait("vector", "cr_st", S.val("cr_st"))
                S.op("vector", "tensor_tensor",
                     ot2[:, 512 * tc:512 * (tc + 1)], bank(b),
                     lt2[:, 512 * tc:512 * (tc + 1)], op=ALU.add)
                S.op("vector", "tensor_tensor",
                     ot2[:, 512 * tc:512 * (tc + 1)],
                     ot2[:, 512 * tc:512 * (tc + 1)],
                     xt2[:, 512 * tc:512 * (tc + 1)], op=ALU.add,
                     inc="cr_ev")
            S.wait("scalar", "cr_ev", S.val("cr_ev"))
            S.op("scalar", "dma_start", out=resid[128 * rt:128 * (rt + 1), :],
                 in_=ot2, inc="cr_st", inc_n=16)
        S.wait("scalar", "cr_st", S.val("cr_st"))
        S.barrier()

        emit_ffn(resid, y_out)
        S.barrier()

        S.build()
    return nc


_NC_CACHE = []


def kernel(L0_attn_x, L0_ffn_w1, L0_ffn_w2, L0_ffn_w3):
    x = np.asarray(L0_attn_x, dtype=np.float32).reshape(8192, 4096)
    w1 = np.ascontiguousarray(np.asarray(L0_ffn_w1, dtype=np.float32))
    w2 = np.ascontiguousarray(np.asarray(L0_ffn_w2, dtype=np.float32))
    w3 = np.ascontiguousarray(np.asarray(L0_ffn_w3, dtype=np.float32))

    j = np.arange(4096, dtype=np.float64)
    dd = np.arange(4096, dtype=np.float64)[:, None]
    ang = 2.0 * np.pi * ((dd * j[None, :]) % 4096) / 4096.0
    cs = np.empty((4096, 4096), dtype=BF)
    cs[:, 0:2049] = np.cos(ang[:, 0:2049]).astype(BF)
    cs[:, 2049:4096] = np.sin(ang[:, 1:2048]).astype(BF)
    kk = np.arange(KPAD, dtype=np.float64)[:, None]
    tt = np.arange(4096, dtype=np.float64)[None, :]
    ck = np.where((kk == 0) | (kk == 2048), 1.0, 2.0) / 4096.0
    ck = np.broadcast_to(ck, (KPAD, 4096)).copy()
    ck[kk[:, 0] > 2048, :] = 0.0
    angk = 2.0 * np.pi * ((kk * tt) % 4096) / 4096.0
    icc = (ck * np.cos(angk)).astype(BF)
    ico = (-ck * np.sin(angk)).astype(BF)
    ident = np.eye(128, dtype=BF)

    if not _NC_CACHE:
        _NC_CACHE.append(build_nc())
    nc = _NC_CACHE[0]

    in_maps = []
    for i in range(NCORES):
        in_maps.append({
            "x": np.ascontiguousarray(x[1024 * i:1024 * (i + 1), :]),
            "w1s": np.ascontiguousarray(w1[:, 512 * i:512 * (i + 1)]),
            "w2s": np.ascontiguousarray(w2[:, 512 * i:512 * (i + 1)]),
            "w3s": np.ascontiguousarray(w3[512 * i:512 * (i + 1), :]),
            "cs": cs, "icc": icc, "ico": ico, "ident": ident,
        })
    res = run_bass_kernel_spmd(nc, in_maps, core_ids=list(range(NCORES)))
    out = np.concatenate([res.results[i]["y"] for i in range(NCORES)], axis=0)
    return out.reshape(1, 8192, 4096)
